# version 14
# baseline (speedup 1.0000x reference)
import os
from contextlib import ExitStack

import numpy as np
import ml_dtypes

import concourse.bass as bass
import concourse.mybir as mybir
from concourse.bass_utils import run_bass_kernel_spmd

F32 = mybir.dt.float32
BF16 = mybir.dt.bfloat16
AF = mybir.ActivationFunctionType
ALU = mybir.AluOpType
AX = mybir.AxisListType
AP = bass.AP

NH, DH, D, SEQ, BATCH = 32, 128, 4096, 4096, 2
TPC = 1024
NCORE = 8
SCALE = DH ** -0.5
WIN = 55.0
EPS = 1e-5


def head_slope(h):
    return 2.0 ** (-h / 4.0)


def head_w(h):
    return 64 if h <= 8 else 256


def build_plan():
    heads = []
    col = 0
    mslots = {64: [], 256: []}
    mindex = {}
    for w in (256, 64):
        for v in range(4):
            for b in range(256 // w):
                qlo_s = 1024 * v + w * b
                qhi_s = 1024 * v + 768 + w * (b + 1)
                lod = qlo_s // 128
                kend = min((qhi_s + 127) // 128, 8 * (v + 1))
                for kt in range(lod, kend):
                    mindex[(w, v, b, kt)] = len(mslots[w])
                    mslots[w].append((v, b, kt))
    for h in range(1, NH + 1):
        s = head_slope(h)
        W = WIN / s
        w = head_w(h)
        blocks = []
        for v in range(4):
            for b in range(256 // w):
                qlo_s = 1024 * v + w * b
                qhi_s = 1024 * v + 768 + w * (b + 1)
                klo = max(0, int((qlo_s - W) // 128))
                kend = min((qhi_s + 127) // 128, 8 * (v + 1))
                lod = qlo_s // 128
                blocks.append(dict(v=v, b=b, klo=klo, kend=kend,
                                   col0=col,
                                   mindex={kt: mindex[(w, v, b, kt)]
                                           for kt in range(max(klo, lod), kend)}))
                col += kend - klo
        heads.append(dict(h=h, w=w, slope=s, blocks=blocks))
    return heads, col, mslots[256], mslots[64]


HEADS, NCOL, M256_SLOTS, M64_SLOTS = build_plan()
M256_BYV = [[i for i, (v, b, kt) in enumerate(M256_SLOTS) if v == vv] for vv in range(4)]
M64_BYV = [[i for i, (v, b, kt) in enumerate(M64_SLOTS) if v == vv] for vv in range(4)]
MASKBUF_COLS = max(max(len(x) for x in M256_BYV) * 256,
                   max(len(x) for x in M64_BYV) * 64)
HEAD_ORDER = [hd for hd in HEADS if hd["w"] == 256] + [hd for hd in HEADS if hd["w"] == 64]


class Builder:
    def __init__(self):
        self.nc = bass.Bass("TRN2", target_bir_lowering=False)
        self.ops = {k: [] for k in ("tensor", "scalar", "vector", "gpsimd", "sync")}
        self.cnt = {}
        self.stack = ExitStack()
        self.sems = {}

    def mksem(self, names):
        for n in names:
            self.sems[n] = self.stack.enter_context(self.nc.semaphore(n))
            self.cnt[n] = 0

    def sb(self, name, shape, dtype):
        return self.stack.enter_context(self.nc.sbuf_tensor(name, shape, dtype))

    def psum(self, name, shape, dtype):
        return self.stack.enter_context(self.nc.psum_tensor(name, shape, dtype))

    def op(self, eng, fn, sem=None, k=1):
        if sem is None:
            self.ops[eng].append(fn)
            return None
        self.cnt[sem] += k
        val = self.cnt[sem]
        h = self.sems[sem]
        self.ops[eng].append(lambda e, f=fn, hh=h, kk=k: f(e).then_inc(hh, kk))
        return val

    def wait(self, eng, sem, val):
        if val is None or val <= 0:
            return
        h = self.sems[sem]
        self.ops[eng].append(lambda e, hh=h, vv=val: e.wait_ge(hh, vv))


def build_program():
    B = Builder()
    nc = B.nc

    def din(name, shape, dtype=F32):
        return nc.dram_tensor(name, list(shape), dtype, kind="ExternalInput").ap()

    x_tok = din("x_tok", (TPC, D))
    x_T = din("x_T", (D, TPC))
    wq = din("wq", (NH, 128, 32, 128))
    wkv = din("wkv", (2, 128, 32, 128))
    wff = din("wff", (32, 128, 32, 128))
    wo = din("wo", (32, 128, 32, 128))
    a_t = din("a_t", (128, 32, 64))
    b_t = din("b_t", (64, D))
    gw_t = din("gw_t", (128, 32, 4))
    bvec = din("bvec", (128, 32))
    gbias = din("gbias", (4, 1))
    ab_t = din("ab_t", (2, NCOL))
    rt_t = din("rt_t", (2, 128))
    m256_t = din("m256_t", (128, len(M256_SLOTS) * 256), BF16)
    m64_t = din("m64_t", (128, len(M64_SLOTS) * 64), BF16)
    ssel_t = din("ssel_t", (4, 64), BF16)

    out_T = nc.dram_tensor("out_T", [D, TPC], F32, kind="ExternalOutput").ap()

    cc_in_k = nc.dram_tensor("cc_in_k", [128, TPC], BF16)
    cc_out_k = nc.dram_tensor("cc_out_k", [4 * 128, TPC], BF16)
    cc_in_v = nc.dram_tensor("cc_in_v", [TPC, 128], BF16)
    cc_out_v = nc.dram_tensor("cc_out_v", [4 * TPC, 128], BF16)

    xn = B.sb("xn", [128, 32 * TPC], BF16)
    attn = B.sb("attn", [128, 32 * TPC], BF16)
    xbuf = B.sb("xbuf", [128, 4096], F32)
    kvb = B.sb("kvb", [128, 4096 + 32 * 129], BF16)
    wpan = B.sb("wpan", [128, 8192], BF16)
    maskb = B.sb("maskb", [128, 2 * MASKBUF_COLS], BF16)
    mult = B.sb("mult", [128, NCOL], F32)
    pbuf = B.sb("pbuf", [128, 4 * 768], BF16)
    q_sb = B.sb("q_sb", [128, 1024], BF16)
    att_stage = B.sb("att_stage", [128, 512], BF16)
    ident_bf = B.sb("ident_bf", [128, 128], BF16)
    ident_f = B.sb("ident_f", [128, 128], F32)
    rt_sb = xbuf[0:2, 1024:1152]
    bmat_sb = B.sb("bmat_sb", [128, 32], F32)
    gb_sb = B.sb("gb_sb", [4, 1], F32)
    ssel_sb = B.sb("ssel_sb", [4, 64], BF16)
    um_sb = B.sb("um_sb", [64, 1024], BF16)
    stats = B.sb("stats", [128, 14], F32)
    epscol = stats[:, 13:14]

    K_sb = kvb[:, 0:4096]
    KVB_PITCH = 4096 + 32 * 129

    def vaug(kt):
        return kvb[:, 4096 + kt * 129: 4096 + (kt + 1) * 129]

    ps_A = B.psum("ps_A", [128, 768], F32)
    ps_B = B.psum("ps_B", [128, 768], F32)
    ps_at0 = B.psum("ps_at0", [128, 512], F32)
    ps_at1 = B.psum("ps_at1", [128, 512], F32)
    ps_q = B.psum("ps_q", [128, 512], F32)
    ps_tr = B.psum("ps_tr", [128, 512], BF16)

    B.mksem(["pe", "act", "dve", "pool", "d_const", "d_x", "d_pan0", "d_pan1",
             "d_mask0", "d_mask1", "d_xt0", "d_xt1", "d_out", "d_kv", "d_ccst",
             "cc", "d_ab0", "d_ab1"])
    C = B.cnt

    for (dst, src) in [(rt_sb[:], rt_t), (bmat_sb[:], bvec), (gb_sb[:], gbias),
                       (ssel_sb[:], ssel_t)]:
        B.op("sync", lambda e, d=dst, s=src: e.dma_start(out=d, in_=s), "d_const", 16)
    v_const = C["d_const"]

    wi1 = B.op("gpsimd", lambda e: e.memset(ident_bf[:], 1.0), "pool")
    B.wait("gpsimd", "pool", wi1)
    B.op("gpsimd", lambda e: e.affine_select(
        ident_bf[:], ident_bf[:], [[1, 128]], ALU.is_equal, 0.0,
        base=0, channel_multiplier=-1), "pool")
    wi2 = B.op("gpsimd", lambda e: e.memset(ident_f[:], 1.0), "pool")
    B.wait("gpsimd", "pool", wi2)
    B.op("gpsimd", lambda e: e.affine_select(
        ident_f[:], ident_f[:], [[1, 128]], ALU.is_equal, 0.0,
        base=0, channel_multiplier=-1), "pool")
    B.op("gpsimd", lambda e: e.memset(kvb[:, 4096:4096 + 32 * 129], 1.0), "pool")
    B.op("gpsimd", lambda e: e.memset(epscol[:], EPS), "pool")
    v_pool_idents = C["pool"]

    nchunks = (NCOL + 511) // 512
    B.wait("tensor", "d_const", v_const)
    for i in range(nchunks):
        c0, c1 = i * 512, min(NCOL, (i + 1) * 512)
        n = c1 - c0
        par = i % 2
        abch = xbuf[0:2, par * 512:(par + 1) * 512]
        dsem = "d_ab1" if par else "d_ab0"
        if i >= 2:
            B.wait("sync", "pe", C["pe"])
        B.op("sync", lambda e, d=abch[0:2, 0:n], s=ab_t[:, c0:c1]:
             e.dma_start(out=d, in_=s), dsem, 16)
        B.wait("tensor", dsem, C[dsem])
        if i >= 1:
            B.wait("tensor", "dve", C["dve"])
        B.op("tensor", lambda e, d=ps_A[:, 0:n], l=rt_sb[:], r=abch[0:2, 0:n]:
             e.matmul(d, l, r, start=True, stop=True), "pe")
        B.wait("vector", "pe", C["pe"])
        B.op("vector", lambda e, d=mult[:, c0:c1], s=ps_A[:, 0:n]:
             e.tensor_scalar(d, s, 0.0, None, ALU.min), "dve")
    B.wait("scalar", "dve", C["dve"])
    B.op("scalar", lambda e: e.activation(mult[:], mult[:], AF.Exp), "act")

    inv_d = 1.0 / D
    QSEM = ["d_x", "d_xt0", "d_xt1", "d_ab0"]
    v_apply = [[0] * 4 for _ in range(8)]
    v_sq = [[0] * 4 for _ in range(8)]
    v_pe_tr = [[0] * 4 for _ in range(8)]
    for t in range(8):
        if t == 0:
            B.wait("sync", "pe", C["pe"])
        vq_loaded = [0] * 4
        for qq in range(4):
            if t > 0:
                B.wait("sync", "dve", v_apply[t - 1][qq])
                B.wait("sync", "act", v_sq[t - 1][qq])
            B.op("sync", lambda e, tt=t, q3=qq: e.dma_start(
                out=xbuf[:, q3 * 1024:(q3 + 1) * 1024],
                in_=x_tok[tt * 128:(tt + 1) * 128, q3 * 1024:(q3 + 1) * 1024]),
                QSEM[qq], 16)
            vq_loaded[qq] = C[QSEM[qq]]
        for qq in range(4):
            B.wait("vector", QSEM[qq], vq_loaded[qq])
            B.op("vector", lambda e, q3=qq: e.reduce_sum(
                stats[:, q3:q3 + 1], xbuf[:, q3 * 1024:(q3 + 1) * 1024], axis=AX.X), "dve")
            B.wait("scalar", QSEM[qq], vq_loaded[qq])
            v_sq[t][qq] = B.op("scalar", lambda e, q3=qq: e.activation(
                attn[:, 8192 + q3 * 1024:8192 + (q3 + 1) * 1024],
                xbuf[:, q3 * 1024:(q3 + 1) * 1024], AF.Square,
                accum_out=stats[:, 4 + q3:5 + q3]), "act")
        vrd = C["dve"]
        B.wait("vector", "dve", vrd)
        v1 = B.op("vector", lambda e: e.tensor_tensor(stats[:, 8:9], stats[:, 0:1], stats[:, 1:2], ALU.add), "dve")
        B.op("vector", lambda e: e.tensor_tensor(stats[:, 10:11], stats[:, 2:3], stats[:, 3:4], ALU.add), "dve")
        B.wait("vector", "dve", C["dve"])
        B.op("vector", lambda e: e.tensor_tensor(stats[:, 8:9], stats[:, 8:9], stats[:, 10:11], ALU.add), "dve")
        B.wait("vector", "act", v_sq[t][3])
        B.op("vector", lambda e: e.tensor_tensor(stats[:, 9:10], stats[:, 4:5], stats[:, 5:6], ALU.add), "dve")
        B.op("vector", lambda e: e.tensor_tensor(stats[:, 10:11], stats[:, 6:7], stats[:, 7:8], ALU.add), "dve")
        B.wait("vector", "dve", C["dve"])
        B.op("vector", lambda e: e.tensor_tensor(stats[:, 9:10], stats[:, 9:10], stats[:, 10:11], ALU.add), "dve")
        B.wait("vector", "dve", C["dve"])
        B.op("vector", lambda e: e.tensor_scalar(stats[:, 8:9], stats[:, 8:9], inv_d, None, ALU.mult), "dve")
        B.op("vector", lambda e: e.tensor_scalar(stats[:, 9:10], stats[:, 9:10], inv_d, None, ALU.mult), "dve")
        B.wait("vector", "dve", C["dve"])
        B.op("vector", lambda e: e.tensor_tensor(stats[:, 10:11], stats[:, 8:9], stats[:, 8:9], ALU.mult), "dve")
        B.wait("vector", "dve", C["dve"])
        B.op("vector", lambda e: e.tensor_tensor(stats[:, 9:10], stats[:, 9:10], stats[:, 10:11], ALU.subtract), "dve")
        vv = C["dve"]
        B.wait("scalar", "dve", vv)
        if t == 0:
            B.wait("scalar", "pool", v_pool_idents)
        va1 = B.op("scalar", lambda e: e.activation(stats[:, 10:11], stats[:, 9:10], AF.Ln, bias=epscol[:]), "act")
        B.wait("scalar", "act", va1)
        B.op("scalar", lambda e: e.activation(stats[:, 11:12], stats[:, 10:11], AF.Exp, scale=-0.5), "act")
        vr = C["act"]
        slot = attn[:, (t % 2) * 4096:(t % 2) * 4096 + 4096]
        B.wait("vector", "act", vr)
        for qq in range(4):
            if t > 1:
                B.wait("vector", "pe", v_pe_tr[t - 2][qq])
            B.op("vector", lambda e, d=slot, q3=qq: e.tensor_scalar(
                d[:, q3 * 1024:(q3 + 1) * 1024], xbuf[:, q3 * 1024:(q3 + 1) * 1024],
                stats[:, 8:9], stats[:, 11:12], ALU.subtract, ALU.mult), "dve")
            v_apply[t][qq] = C["dve"]
            B.wait("tensor", "dve", v_apply[t][qq])
            for c0 in range(8 * qq, 8 * qq + 8, 4):
                B.wait("tensor", "dve", C["dve"])
                for j in range(4):
                    B.op("tensor", lambda e, s=slot[:, (c0 + j) * 128:(c0 + j + 1) * 128],
                         d=ps_tr[:, j * 128:(j + 1) * 128]:
                         e.transpose(d, s, ident_bf[:]), "pe")
                vp = C["pe"]
                B.wait("vector", "pe", vp)
                dst = AP(xn, c0 * 1024 + t * 128, [[32 * TPC, 128], [1024, 4], [1, 128]])
                B.op("vector", lambda e, d=dst: e.tensor_copy(d, ps_tr[:, 0:512]), "dve")
            v_pe_tr[t][qq] = C["pe"]
    v_p1_dve = C["dve"]

    B.wait("gpsimd", "pool", v_pool_idents)
    for f in range(2):
        B.op("gpsimd", lambda e, ff=f: e.dma_start(
            out=wpan[:, ff * 4096:(ff + 1) * 4096].rearrange("p (c f) -> p c f", c=32),
            in_=wkv[ff]), "d_pan0", 16)
    v_kvpan = C["d_pan0"]
    K_loc = attn[:, 0:1024]
    V_locT = attn[:, 1024:2048]
    V_tok = attn[:, 2048:3072]
    B.wait("tensor", "d_pan0", v_kvpan)
    B.wait("tensor", "dve", v_p1_dve)
    for f in range(2):
        for tb in range(2):
            for c in range(32):
                B.op("tensor", lambda e, ff=f, cc2=c, tt=tb: e.matmul(
                    ps_A[:, 0:512],
                    wpan[:, ff * 4096 + cc2 * 128: ff * 4096 + (cc2 + 1) * 128],
                    xn[:, cc2 * 1024 + tt * 512: cc2 * 1024 + (tt + 1) * 512],
                    start=(cc2 == 0), stop=(cc2 == 31)), "pe")
            vmm = C["pe"]
            B.wait("vector", "pe", vmm)
            dst = (K_loc if f == 0 else V_locT)[:, tb * 512:(tb + 1) * 512]
            B.op("vector", lambda e, d=dst: e.tensor_copy(d, ps_A[:, 0:512]), "dve")
            B.wait("tensor", "dve", C["dve"])
    for j0 in range(0, 8, 4):
        if j0 > 0:
            B.wait("tensor", "dve", C["dve"])
        for j in range(4):
            B.op("tensor", lambda e, s=V_locT[:, (j0 + j) * 128:(j0 + j + 1) * 128],
                 d=ps_tr[:, j * 128:(j + 1) * 128]:
                 e.transpose(d, s, ident_bf[:]), "pe")
        vp = C["pe"]
        B.wait("vector", "pe", vp)
        B.op("vector", lambda e, d=V_tok[:, j0 * 128:(j0 + 4) * 128]:
             e.tensor_copy(d, ps_tr[:, 0:512]), "dve")
    v_kv_done = C["dve"]
    B.wait("sync", "dve", v_kv_done)
    B.op("sync", lambda e: e.dma_start(out=cc_in_k.ap(), in_=K_loc), "d_ccst", 16)
    B.op("sync", lambda e: e.dma_start(
        out=cc_in_v.ap().rearrange("(j p) f -> p j f", p=128),
        in_=V_tok.rearrange("p (j f) -> p j f", j=8)), "d_ccst", 16)
    if os.environ.get("KDBG"):
        dbg_kloc = nc.dram_tensor("dbg_kloc", [128, 1024], BF16, kind="ExternalOutput").ap()
        dbg_vtok = nc.dram_tensor("dbg_vtok", [128, 1024], BF16, kind="ExternalOutput").ap()
        B.op("sync", lambda e: e.dma_start(out=dbg_kloc, in_=K_loc), "d_ccst", 16)
        B.op("sync", lambda e: e.dma_start(out=dbg_vtok, in_=V_tok), "d_ccst", 16)
    v_ccst = C["d_ccst"]
    B.wait("gpsimd", "d_ccst", v_ccst)
    B.op("gpsimd", lambda e: e.collective_compute(
        "AllGather", ALU.bypass, replica_groups=[[0, 1, 2, 3], [4, 5, 6, 7]],
        ins=[cc_in_k.ap().opt()], outs=[cc_out_k.ap().opt()]), "cc")
    B.op("gpsimd", lambda e: e.collective_compute(
        "AllGather", ALU.bypass, replica_groups=[[0, 1, 2, 3], [4, 5, 6, 7]],
        ins=[cc_in_v.ap().opt()], outs=[cc_out_v.ap().opt()]), "cc")
    v_cc = C["cc"]
    B.wait("sync", "cc", v_cc)
    for cseq in range(16):
        if cseq < 4:
            r, v = cseq, 0
        elif cseq < 8:
            r, v = 7 - cseq, 1
        elif cseq < 12:
            r, v = cseq - 8, 2
        else:
            r, v = 15 - cseq, 3
        B.op("sync", lambda e, d=K_sb[:, 256 * cseq:256 * (cseq + 1)],
             s=cc_out_k.ap()[128 * r:128 * (r + 1), 256 * v:256 * (v + 1)]:
             e.dma_start(out=d, in_=s), "d_kv", 16)
        dstv = AP(kvb, 4096 + (2 * cseq) * 129, [[KVB_PITCH, 128], [129, 2], [1, 128]])
        B.op("sync", lambda e, d=dstv,
             s=cc_out_v.ap()[1024 * r + 256 * v:1024 * r + 256 * (v + 1), :]
             .rearrange("(j p) f -> p j f", p=128):
             e.dma_start(out=d, in_=s), "d_kv", 16)
    v_kvload = C["d_kv"]

    A_sb = wpan[:, 0:2048]
    gw_sb = wpan[:, 2048:2176]
    B.wait("gpsimd", "pe", C["pe"])
    B.op("gpsimd", lambda e: e.dma_start(
        out=A_sb.rearrange("p (c r) -> p c r", c=32), in_=a_t), "d_pan1", 16)
    B.op("gpsimd", lambda e: e.dma_start(
        out=gw_sb.rearrange("p (c j) -> p c j", c=32), in_=gw_t), "d_pan1", 16)
    v_p3pan = C["d_pan1"]
    gate_T = xbuf[0:4, 0:1024]
    gtok = xbuf[:, 1024:1056]
    mstack = xbuf[:, 1056:1088]
    B.wait("tensor", "d_pan1", v_p3pan)
    for tb in range(2):
        for c in range(32):
            B.op("tensor", lambda e, cc2=c, tt=tb: e.matmul(
                ps_q[0:4, 0:512], gw_sb[:, cc2 * 4:(cc2 + 1) * 4],
                xn[:, cc2 * 1024 + tt * 512: cc2 * 1024 + (tt + 1) * 512],
                start=(cc2 == 0), stop=(cc2 == 31)), "pe")
        vg = C["pe"]
        B.wait("vector", "pe", vg)
        B.wait("vector", "d_const", v_const)
        B.op("vector", lambda e, d=gate_T[0:4, tb * 512:(tb + 1) * 512]:
             e.tensor_scalar(d, ps_q[0:4, 0:512], gb_sb[:], None, ALU.add), "dve")
        B.wait("tensor", "dve", C["dve"])
    B.op("gpsimd", lambda e: e.memset(mstack, 1.0), "pool")
    v_mset = C["pool"]
    B.wait("vector", "pool", v_mset)
    for t in range(8):
        if t > 0:
            B.wait("tensor", "dve", C["dve"])
        B.op("tensor", lambda e, s=gate_T[0:4, t * 128:(t + 1) * 128]:
             e.transpose(ps_q[:, 256:260], s, ident_f[0:4, 0:4]), "pe")
        vp = C["pe"]
        B.wait("vector", "pe", vp)
        vcp = B.op("vector", lambda e, d=gtok[:, t * 4:(t + 1) * 4]:
             e.tensor_copy(d, ps_q[:, 256:260]), "dve")
        B.wait("vector", "dve", vcp)
        g0 = gtok[:, t * 4 + 0:t * 4 + 1]
        g1 = gtok[:, t * 4 + 1:t * 4 + 2]
        g2 = gtok[:, t * 4 + 2:t * 4 + 3]
        g3 = gtok[:, t * 4 + 3:t * 4 + 4]
        sc = xbuf[:, 1088 + t * 4:1092 + t * 4]
        ms = mstack[:, t * 4:(t + 1) * 4]
        w1 = B.op("vector", lambda e, a=sc[:, 0:1], x=g1, y=g2: e.tensor_tensor(a, x, y, ALU.max), "dve")
        B.wait("vector", "dve", w1)
        w2 = B.op("vector", lambda e, a=sc[:, 0:1], y=g3: e.tensor_tensor(a, a, y, ALU.max), "dve")
        B.wait("vector", "dve", w2)
        B.op("vector", lambda e, d=ms[:, 1:2], x=sc[:, 0:1], y=g0: e.tensor_tensor(d, x, y, ALU.is_gt), "dve")
        w3 = B.op("vector", lambda e, a=sc[:, 1:2], x=g2, y=g3: e.tensor_tensor(a, x, y, ALU.max), "dve")
        w4 = B.op("vector", lambda e, a=sc[:, 2:3], x=g0, y=g1: e.tensor_tensor(a, x, y, ALU.max), "dve")
        B.wait("vector", "dve", w4)
        B.op("vector", lambda e, d=ms[:, 2:3], x=sc[:, 1:2], y=sc[:, 2:3]: e.tensor_tensor(d, x, y, ALU.is_gt), "dve")
        w5 = B.op("vector", lambda e, a=sc[:, 3:4], x=sc[:, 2:3], y=g2: e.tensor_tensor(a, x, y, ALU.max), "dve")
        B.wait("vector", "dve", w5)
        B.op("vector", lambda e, d=ms[:, 3:4], x=g3, y=sc[:, 3:4]: e.tensor_tensor(d, x, y, ALU.is_gt), "dve")
    v_mstack = C["dve"]
    msk_T = attn[0:4, 3072:4096]
    B.wait("tensor", "dve", v_mstack)
    for t in range(8):
        if t > 0:
            B.wait("tensor", "dve", C["dve"])
        B.op("tensor", lambda e, s=mstack[:, t * 4:(t + 1) * 4]:
             e.transpose(ps_q[0:4, 256:384], s, ident_f[:]), "pe")
        vp = C["pe"]
        B.wait("vector", "pe", vp)
        B.op("vector", lambda e, d=msk_T[0:4, t * 128:(t + 1) * 128]:
             e.tensor_copy(d, ps_q[0:4, 256:384]), "dve")
    v_mskT = C["dve"]
    B.wait("tensor", "dve", v_mskT)
    for tb in range(2):
        for c in range(32):
            B.op("tensor", lambda e, cc2=c, tt=tb: e.matmul(
                ps_A[0:64, 0:512], A_sb[:, cc2 * 64:(cc2 + 1) * 64],
                xn[:, cc2 * 1024 + tt * 512: cc2 * 1024 + (tt + 1) * 512],
                start=(cc2 == 0), stop=(cc2 == 31)), "pe")
        B.op("tensor", lambda e, tt=tb: e.matmul(
            ps_B[0:64, 0:512], ssel_sb[:], msk_T[0:4, tt * 512:(tt + 1) * 512],
            start=True, stop=True), "pe")
        vu = C["pe"]
        B.wait("vector", "pe", vu)
        w6 = B.op("vector", lambda e, d=um_sb[0:64, tb * 512:(tb + 1) * 512]:
             e.tensor_copy(d, ps_B[0:64, 0:512]), "dve")
        B.wait("vector", "dve", w6)
        B.op("vector", lambda e, d=um_sb[0:64, tb * 512:(tb + 1) * 512]:
             e.tensor_tensor(d, ps_A[0:64, 0:512], d, ALU.mult), "dve")
        B.wait("tensor", "dve", C["dve"])
    v_p3_done = C["dve"]

    v_pan = [0, 0]

    def emit_panel_load(hidx, par, wait_pe):
        hd = HEAD_ORDER[hidx]
        B.wait("gpsimd", "pe", wait_pe)
        B.op("gpsimd", lambda e, s=wq[hd["h"] - 1], pp=par: e.dma_start(
            out=wpan[:, pp * 4096:(pp + 1) * 4096].rearrange("p (c f) -> p c f", c=32),
            in_=s), "d_pan%d" % par, 16)
        v_pan[par] = C["d_pan%d" % par]

    mask_state = {}
    mask_info = {}
    par_m = [0]

    def emit_mask_load(w, v):
        p = par_m[0]
        par_m[0] = 1 - p
        old = mask_state.get(p)
        if old is not None:
            mask_info.pop(old, None)
        B.wait("sync", "dve", C["dve"])
        slots = M256_BYV[v] if w == 256 else M64_BYV[v]
        src_t = m256_t if w == 256 else m64_t
        n = len(slots) * w
        s0 = slots[0]
        B.op("sync", lambda e, d=maskb[:, p * MASKBUF_COLS: p * MASKBUF_COLS + n],
             s=src_t[:, s0 * w: s0 * w + n]:
             e.dma_start(out=d, in_=s), "d_mask%d" % p, 16)
        cmap = {sl: p * MASKBUF_COLS + (sl - s0) * w for sl in slots}
        mask_state[p] = (w, v)
        mask_info[(w, v)] = (C["d_mask%d" % p], cmap, p)

    grp_exp_done = []
    grp_mult_done = []
    grp_pv_done = {}
    pend_pv = []
    state = {"qevac": v_p3_done, "last_copy": 0, "ev": 0, "stage_free": 0,
             "bank_evac": [0, 0]}
    pend_fin = []
    at_banks = [ps_at0, ps_at1]

    def emit_evac(ev):
        mq, bank, col0 = ev["mq"], ev["bank"], ev["col0"]
        bk = at_banks[bank]
        B.wait("vector", "pe", ev["pvwait"])
        B.wait("vector", "pe", state["stage_free"])
        w7 = B.op("vector", lambda e, d=stats[0:mq, 12:13], s=bk[0:mq, col0 + 128:col0 + 129]:
                  e.reciprocal(d, s), "dve")
        B.wait("vector", "dve", w7)
        sc = ev["stage_col"]
        vts = B.op("vector", lambda e, d=att_stage[0:mq, sc:sc + 128],
                   s=bk[0:mq, col0:col0 + 128], r=stats[0:mq, 12:13]:
                   e.tensor_scalar(d, s, r, None, ALU.mult), "dve")
        state["bank_evac"][bank] = vts
        pend_fin.append((sc, mq, ev["dst"], vts))

    def flush_evac_fin():
        for (sc, mq, dst, vts) in pend_fin:
            B.wait("tensor", "dve", vts)
            B.wait("tensor", "dve", state["last_copy"])
            rot = state["ev"] % 2
            state["ev"] += 1
            B.op("tensor", lambda e, s=att_stage[0:mq, sc:sc + 128],
                 d=ps_tr[:, rot * 128:rot * 128 + mq], m=mq:
                 e.transpose(d, s, ident_bf[0:m, 0:m]), "pe")
            vtr = C["pe"]
            B.wait("vector", "pe", vtr)
            B.op("vector", lambda e, d=attn[:, dst:dst + mq],
                 s=ps_tr[:, rot * 128:rot * 128 + mq]: e.tensor_copy(d, s), "dve")
            state["last_copy"] = C["dve"]
        if pend_fin:
            state["stage_free"] = C["pe"]
        pend_fin.clear()

    def emit_pv_batch(item):
        B.wait("tensor", "dve", item["mult_done"])
        if item["memset_wait"]:
            B.wait("tensor", "dve", item["memset_wait"])
        for (post_col, mq, bank, outcol, kt) in item["pvs"]:
            B.op("tensor", lambda e, d=at_banks[bank][0:mq, outcol:outcol + 129],
                 l=pbuf[:, post_col:post_col + mq], r=vaug(kt):
                 e.matmul(d, l, r, start=False, stop=False, skip_group_check=True), "pe")
        pvd = C["pe"]
        grp_pv_done[item["gi"]] = pvd
        for ev in item["evacs"]:
            ev["pvwait"] = pvd
            emit_evac(ev)

    def flush_pv(upto):
        while len(pend_pv) > upto:
            emit_pv_batch(pend_pv.pop(0))

    def emit_group(gi, tiles, qr, dve_ops, pvs, evacs, memset_wait):
        gw_cols = len(tiles) * 256
        bankA = (gi % 2 == 0)
        psb = ps_A if bankA else ps_B
        post_off = (gi % 4) * 768
        pre = pbuf[:, post_off:post_off + 768]
        if gi >= 2:
            B.wait("tensor", "act", grp_exp_done[gi - 2])
        B.wait("tensor", "dve", state["qevac"])
        for ji, kt in enumerate(tiles):
            B.op("tensor", lambda e, d=psb[:, ji * 256:(ji + 1) * 256],
                 l=K_sb[:, kt * 128:(kt + 1) * 128], r=qr:
                 e.matmul(d, l, r, start=True, stop=True), "pe")
        v_sc = C["pe"]
        B.wait("scalar", "pe", v_sc)
        if gi >= 4:
            B.wait("scalar", "pe", grp_pv_done[gi - 4])
        B.op("scalar", lambda e, d=pre[:, 0:gw_cols], s=psb[:, 0:gw_cols]:
             e.activation(d, s, AF.Exp, scale=SCALE), "act")
        grp_exp_done.append(C["act"])
        B.wait("vector", "act", grp_exp_done[-1])
        for (ji, qoff, qw, colx, moff) in dve_ops:
            mcol = mult[:, colx:colx + 1]
            dd = pre[:, ji * 256 + qoff: ji * 256 + qoff + qw]
            if moff is not None:
                B.op("vector", lambda e, d=dd, s=dd, m=maskb[:, moff:moff + qw], c2=mcol:
                     e.scalar_tensor_tensor(d, s, c2, m, ALU.mult, ALU.mult), "dve")
            else:
                B.op("vector", lambda e, d=dd, s=dd, c2=mcol:
                     e.tensor_scalar(d, s, c2, None, ALU.mult), "dve")
        grp_mult_done.append(C["dve"])
        pend_pv.append(dict(gi=gi, mult_done=grp_mult_done[-1], memset_wait=memset_wait,
                            pvs=[(post_off + pc, mq, bank, oc, kt) for (pc, mq, bank, oc, kt) in pvs],
                            evacs=evacs))
        flush_pv(3)

    gi_all = 0
    emit_mask_load(HEAD_ORDER[0]["w"], 0)
    emit_panel_load(0, 0, C["pe"])

    for hidx, hd in enumerate(HEAD_ORDER):
        h, w, slope = hd["h"], hd["w"], hd["slope"]
        par = hidx % 2
        if hidx > 0 and HEAD_ORDER[hidx - 1]["w"] == 64:
            flush_pv(0)
            flush_evac_fin()
        B.wait("tensor", "d_pan%d" % par, v_pan[par])
        for qr_i in range(2):
            qps = ps_q if qr_i == 0 else ps_at0
            if qr_i == 0:
                B.wait("tensor", "dve", state["qevac"])
            else:
                B.wait("tensor", "dve", state["bank_evac"][0])
            for c in range(32):
                B.op("tensor", lambda e, cc2=c, rr=qr_i, pp=par, d=qps: e.matmul(
                    d[:, 0:512],
                    wpan[:, pp * 4096 + cc2 * 128: pp * 4096 + (cc2 + 1) * 128],
                    xn[:, cc2 * 1024 + rr * 512: cc2 * 1024 + rr * 512 + 512],
                    start=(cc2 == 0), stop=(cc2 == 31)), "pe")
            vq = C["pe"]
            B.wait("vector", "pe", vq)
            B.op("vector", lambda e, d=q_sb[:, qr_i * 512:(qr_i + 1) * 512], s=qps:
                 e.tensor_copy(d, s[:, 0:512]), "dve")
            state["qevac"] = C["dve"]
        if hidx + 1 < len(HEAD_ORDER):
            emit_panel_load(hidx + 1, 1 - par, C["pe"])
        if hidx == 0:
            B.wait("tensor", "d_kv", v_kvload)
        for v in range(4):
            if (w, v) not in mask_info:
                emit_mask_load(w, v)
            if v < 3 and (w, v + 1) not in mask_info:
                emit_mask_load(w, v + 1)
            msemv, mcolmap, mp = mask_info[(w, v)]
            B.wait("vector", "d_mask%d" % mp, msemv)
            qr = q_sb[:, v * 256:(v + 1) * 256]
            blks = [bb for bb in hd["blocks"] if bb["v"] == v]
            if w == 256:
                blk = blks[0]
                bank = v % 2
                ms = B.op("vector", lambda e, d=at_banks[bank][:, 0:258]: e.memset(d, 0.0), "dve")
                klist = list(range(blk["klo"], blk["kend"]))
                groups = [klist[i:i + 3] for i in range(0, len(klist), 3)]
                for gj, tiles in enumerate(groups):
                    dve_ops = []
                    pvs = []
                    for ji, kt in enumerate(tiles):
                        colx = blk["col0"] + (kt - blk["klo"])
                        moff = mcolmap[blk["mindex"][kt]] if kt in blk["mindex"] else None
                        dve_ops.append((ji, 0, 256, colx, moff))
                        for qs in range(2):
                            pvs.append((ji * 256 + qs * 128, 128, bank, qs * 129, kt))
                    evacs = []
                    if gj == len(groups) - 1:
                        for qs in range(2):
                            evacs.append(dict(mq=128, bank=bank, col0=qs * 129,
                                              stage_col=(v % 2) * 256 + qs * 128,
                                              dst=(h - 1) * 1024 + v * 256 + qs * 128))
                    emit_group(gi_all, tiles, qr, dve_ops, pvs, evacs, ms)
                    gi_all += 1
                    if gj == 0:
                        flush_evac_fin()
            else:
                flush_pv(0)
                ms0 = B.op("vector", lambda e, d=at_banks[0][0:64, 0:258]: e.memset(d, 0.0), "dve")
                ms1 = B.op("vector", lambda e, d=at_banks[1][0:64, 0:258]: e.memset(d, 0.0), "dve")
                klo_u = min(bb["klo"] for bb in blks)
                kend_u = max(bb["kend"] for bb in blks)
                klist = list(range(klo_u, kend_u))
                groups = [klist[i:i + 3] for i in range(0, len(klist), 3)]
                last_kt = {bb["b"]: bb["kend"] - 1 for bb in blks}
                for gj, tiles in enumerate(groups):
                    dve_ops = []
                    pvs = []
                    evacs = []
                    for ji, kt in enumerate(tiles):
                        for bb in blks:
                            b = bb["b"]
                            if not (bb["klo"] <= kt < bb["kend"]):
                                continue
                            colx = bb["col0"] + (kt - bb["klo"])
                            moff = mcolmap[bb["mindex"][kt]] if kt in bb["mindex"] else None
                            dve_ops.append((ji, b * 64, 64, colx, moff))
                            pvs.append((ji * 256 + b * 64, 64, b // 2, (b % 2) * 129, kt))
                            if kt == last_kt[b]:
                                evacs.append(dict(mq=64, bank=b // 2, col0=(b % 2) * 129,
                                                  stage_col=b * 128,
                                                  dst=(h - 1) * 1024 + v * 256 + b * 64))
                    emit_group(gi_all, tiles, qr, dve_ops, pvs, evacs, max(ms0, ms1))
                    gi_all += 1
                    if gj == 0:
                        flush_evac_fin()
        if hidx + 1 < len(HEAD_ORDER):
            nw = HEAD_ORDER[hidx + 1]["w"]
            if (nw, 0) not in mask_info:
                emit_mask_load(nw, 0)
    flush_pv(0)
    flush_evac_fin()
    v_attn_done = C["dve"]

    B_sbuf = maskb[0:64, 0:4096]
    B.wait("gpsimd", "dve", C["dve"])
    B.op("gpsimd", lambda e: e.dma_start(out=B_sbuf, in_=b_t), "d_mask0", 16)
    v_bt = C["d_mask0"]

    v_pan5 = [0, 0]

    def load_out_panels(ot, par, wait_pe):
        B.wait("gpsimd", "pe", wait_pe)
        B.op("gpsimd", lambda e, s=wo[ot], pp=par: e.dma_start(
            out=wpan[:, pp * 4096:(pp + 1) * 4096].rearrange("p (c f) -> p c f", c=32),
            in_=s), "d_pan%d" % par, 16)
        B.op("gpsimd", lambda e, s=wff[ot], pp=par: e.dma_start(
            out=kvb[:, pp * 4096:(pp + 1) * 4096].rearrange("p (c f) -> p c f", c=32),
            in_=s), "d_pan%d" % par, 16)
        v_pan5[par] = C["d_pan%d" % par]

    v_xt = [0, 0]

    def load_xt(ot, par, wait_dve):
        B.wait("sync", "dve", wait_dve)
        B.op("sync", lambda e, s=x_T[ot * 128:(ot + 1) * 128, :], pp=par:
             e.dma_start(out=xbuf[:, pp * 1024:(pp + 1) * 1024], in_=s), "d_xt%d" % par, 16)
        v_xt[par] = C["d_xt%d" % par]

    load_out_panels(0, 0, C["pe"])
    load_xt(0, 0, C["dve"])
    n_out = 0
    for ot in range(32):
        par = ot % 2
        if ot + 1 < 32:
            load_out_panels(ot + 1, 1 - par, C["pe"])
            load_xt(ot + 1, 1 - par, C["dve"])
        B.wait("tensor", "d_pan%d" % par, v_pan5[par])
        if ot == 0:
            B.wait("tensor", "dve", v_attn_done)
            B.wait("tensor", "d_mask0", v_bt)
        for tb in range(2):
            psb = ps_A if tb == 0 else ps_B
            B.wait("tensor", "dve", C["dve"])
            for c in range(32):
                B.op("tensor", lambda e, cc2=c, tt=tb, pp=par: e.matmul(
                    psb[:, 0:512] if False else (ps_A if tt == 0 else ps_B)[:, 0:512],
                    wpan[:, pp * 4096 + cc2 * 128: pp * 4096 + (cc2 + 1) * 128],
                    attn[:, cc2 * 1024 + tt * 512: cc2 * 1024 + (tt + 1) * 512],
                    start=(cc2 == 0), stop=False), "pe")
            for c in range(32):
                B.op("tensor", lambda e, cc2=c, tt=tb, pp=par: e.matmul(
                    (ps_A if tt == 0 else ps_B)[:, 0:512],
                    kvb[:, pp * 4096 + cc2 * 128: pp * 4096 + (cc2 + 1) * 128],
                    xn[:, cc2 * 1024 + tt * 512: cc2 * 1024 + (tt + 1) * 512],
                    start=False, stop=False), "pe")
            B.op("tensor", lambda e, oo=ot, tt=tb: e.matmul(
                (ps_A if tt == 0 else ps_B)[:, 0:512],
                B_sbuf[0:64, oo * 128:(oo + 1) * 128],
                um_sb[0:64, tt * 512:(tt + 1) * 512],
                start=False, stop=True), "pe")
            vmm = C["pe"]
            B.wait("vector", "pe", vmm)
            B.wait("vector", "d_xt%d" % par, v_xt[par])
            outsb = xbuf[:, 3072 + tb * 512: 3072 + (tb + 1) * 512]
            if n_out >= 2:
                B.wait("vector", "d_out", 16 * n_out)
            B.op("vector", lambda e, d=outsb, s=psb[:, 0:512], bcol=bmat_sb[:, ot:ot + 1],
                 xr=xbuf[:, par * 1024 + tb * 512: par * 1024 + (tb + 1) * 512]:
                 e.scalar_tensor_tensor(d, s, bcol, xr, ALU.add, ALU.add), "dve")
            vout = C["dve"]
            B.wait("sync", "dve", vout)
            B.op("sync", lambda e, s=outsb,
                 d=out_T[ot * 128:(ot + 1) * 128, tb * 512:(tb + 1) * 512]:
                 e.dma_start(out=d, in_=s), "d_out", 16)
            n_out += 1
    if os.environ.get("KDBG"):
        dbgs = [
            ("dbg_xn", xn[:, 0:1024], BF16, [128, 1024]),
            ("dbg_k", K_sb[:, 0:1024], BF16, [128, 1024]),
            ("dbg_v", kvb[:, 4096:4096 + 258], BF16, [128, 258]),
            ("dbg_q", q_sb[:], BF16, [128, 1024]),
            ("dbg_mult", mult[:, 0:512], F32, [128, 512]),
            ("dbg_attn_h9", attn[:, 8 * 1024:9 * 1024], BF16, [128, 1024]),
            ("dbg_attn_h1", attn[:, 0:1024], BF16, [128, 1024]),
            ("dbg_um", um_sb[:], BF16, [64, 1024]),
            ("dbg_gt", xbuf[:, 1024:1088], F32, [128, 64]),
            ("dbg_ppost", ppost[:], BF16, [128, 1536]),
            ("dbg_stats", stats[:], F32, [128, 16]),
        ]
        B.wait("sync", "dve", C["dve"])
        B.wait("sync", "pool", C["pool"])
        B.wait("sync", "act", C["act"])
        B.wait("sync", "pe", C["pe"])
        nd = 0
        for name, ap_, dt, shp in dbgs:
            o = nc.dram_tensor(name, shp, dt, kind="ExternalOutput").ap()
            B.op("sync", lambda e, d=o, s=ap_: e.dma_start(out=d, in_=s), "d_out", 16)
            nd += 1
        n_out += nd
    B.wait("sync", "d_out", 16 * n_out)

    with nc.Block() as block:
        @block.tensor
        def _(t):
            for f in B.ops["tensor"]:
                f(t)

        @block.scalar
        def _(s):
            for f in B.ops["scalar"]:
                f(s)

        @block.vector
        def _(v):
            for f in B.ops["vector"]:
                f(v)

        @block.gpsimd
        def _(g):
            for f in B.ops["gpsimd"]:
                f(g)

        @block.sync
        def _(sy):
            for f in B.ops["sync"]:
                f(sy)

    B.stack.close()
    return nc



def _stage_weights(Wqkv, Wo, W, A, Bm, gate_w, b, gate_b):
    f32 = np.float32
    wq_st = np.ascontiguousarray(
        np.transpose(Wqkv[:, :4096].reshape(32, 128, 32, 128), (2, 1, 0, 3)), dtype=f32)
    wkv_st = np.ascontiguousarray(
        np.transpose(Wqkv[:, 4096:].reshape(32, 128, 2, 128), (2, 1, 0, 3)), dtype=f32)
    wff_st = np.ascontiguousarray(
        np.transpose(W.reshape(32, 128, 32, 128), (0, 3, 2, 1)), dtype=f32)
    wo_st = np.ascontiguousarray(
        np.transpose(Wo.reshape(32, 128, 32, 128), (2, 1, 0, 3)), dtype=f32)
    a_st = np.ascontiguousarray(np.transpose(A.reshape(64, 32, 128), (2, 1, 0)), dtype=f32)
    b_st = np.ascontiguousarray(Bm.T, dtype=f32)
    gw_st = np.ascontiguousarray(np.transpose(gate_w.reshape(4, 32, 128), (2, 1, 0)), dtype=f32)
    bvec_st = np.ascontiguousarray(b.reshape(32, 128).T, dtype=f32)
    gb_st = np.ascontiguousarray(gate_b.reshape(4, 1), dtype=f32)
    return wq_st, wkv_st, wff_st, wo_st, a_st, b_st, gw_st, bvec_st, gb_st


def _core_tables(chunks):
    ab = np.zeros((2, NCOL), np.float32)
    for hd in HEADS:
        s, w = hd["slope"], hd["w"]
        for blk in hd["blocks"]:
            v, b = blk["v"], blk["b"]
            q_end = 256 * chunks[v] + w * (b + 1)
            for i, kt in enumerate(range(blk["klo"], blk["kend"])):
                ab[0, blk["col0"] + i] = s * (128 * kt - q_end)
                ab[1, blk["col0"] + i] = s
    bf16 = ml_dtypes.bfloat16
    m256 = np.zeros((128, len(M256_SLOTS) * 256), bf16)
    for i, (v, b, kt) in enumerate(M256_SLOTS):
        keys = 128 * kt + np.arange(128)[:, None]
        qs = 256 * chunks[v] + 256 * b + np.arange(256)[None, :]
        m256[:, i * 256:(i + 1) * 256] = (keys <= qs).astype(bf16)
    m64 = np.zeros((128, len(M64_SLOTS) * 64), bf16)
    for i, (v, b, kt) in enumerate(M64_SLOTS):
        keys = 128 * kt + np.arange(128)[:, None]
        qs = 256 * chunks[v] + 64 * b + np.arange(64)[None, :]
        m64[:, i * 64:(i + 1) * 64] = (keys <= qs).astype(bf16)
    return ab, m256, m64


def kernel(L0_attn_hidden_states, L0_attn_Wqkv, L0_attn_Wo, L0_ffn_W, L0_ffn_b,
           L0_ffn_A, L0_ffn_B, L0_ffn_gate_w, L0_ffn_gate_b):
    x = np.asarray(L0_attn_hidden_states, np.float32)
    (wq_st, wkv_st, wff_st, wo_st, a_st, b_st, gw_st, bvec_st, gb_st) = _stage_weights(
        np.asarray(L0_attn_Wqkv, np.float32), np.asarray(L0_attn_Wo, np.float32),
        np.asarray(L0_ffn_W, np.float32), np.asarray(L0_ffn_A, np.float32),
        np.asarray(L0_ffn_B, np.float32), np.asarray(L0_ffn_gate_w, np.float32),
        np.asarray(L0_ffn_b, np.float32), np.asarray(L0_ffn_gate_b, np.float32))
    rt_st = np.stack([np.ones(128, np.float32),
                      np.arange(128, dtype=np.float32)])
    ssel = np.zeros((4, 64), ml_dtypes.bfloat16)
    ssel[0, 0:8] = 1
    ssel[1, 8:16] = 1
    ssel[2, 16:32] = 1
    ssel[3, 32:64] = 1

    nc = build_program()

    in_maps = []
    tok_idx_all = []
    for slot in range(NCORE):
        bt, g = slot // 4, slot % 4
        chunks = [g, 7 - g, 8 + g, 15 - g]
        tok_idx = np.concatenate([np.arange(256 * c, 256 * (c + 1)) for c in chunks])
        tok_idx_all.append((bt, tok_idx))
        x_core = np.ascontiguousarray(x[bt, tok_idx, :])
        xT_core = np.ascontiguousarray(x_core.T)
        ab, m256, m64 = _core_tables(chunks)
        in_maps.append({
            "x_tok": x_core, "x_T": xT_core,
            "wq": wq_st, "wkv": wkv_st, "wff": wff_st, "wo": wo_st,
            "a_t": a_st, "b_t": b_st, "gw_t": gw_st, "bvec": bvec_st, "gbias": gb_st,
            "ab_t": ab, "rt_t": rt_st, "m256_t": m256, "m64_t": m64, "ssel_t": ssel,
        })

    res = run_bass_kernel_spmd(nc, in_maps, core_ids=list(range(NCORE)))

    out = np.empty((BATCH, SEQ, D), np.float32)
    for slot in range(NCORE):
        bt, tok_idx = tok_idx_all[slot]
        out[bt, tok_idx, :] = res.results[slot]["out_T"].T
    return out


if __name__ == "__main__":
    build_program()
    print("program built ok")


# version 15
# speedup vs baseline: 1.0077x; 1.0077x over previous
import os
from contextlib import ExitStack

import numpy as np
import ml_dtypes

import concourse.bass as bass
import concourse.bass_utils as _bass_utils
import concourse.mybir as mybir
from concourse.bass_utils import run_bass_kernel_spmd

if not getattr(_bass_utils, "_ldw_patched", False):
    _orig_run_command = _bass_utils.run_command

    def _run_command_ldw(argv, **kw):
        argv = ["--enable-ldw-opt=true" if a == "--enable-ldw-opt=false" else a
                for a in argv]
        return _orig_run_command(argv, **kw)

    _bass_utils.run_command = _run_command_ldw
    _bass_utils._ldw_patched = True

F32 = mybir.dt.float32
BF16 = mybir.dt.bfloat16
AF = mybir.ActivationFunctionType
ALU = mybir.AluOpType
AX = mybir.AxisListType
AP = bass.AP

NH, DH, D, SEQ, BATCH = 32, 128, 4096, 4096, 2
TPC = 1024
NCORE = 8
SCALE = DH ** -0.5
WIN = 55.0
EPS = 1e-5


def head_slope(h):
    return 2.0 ** (-h / 4.0)


def head_w(h):
    return 64 if h <= 8 else 256


def build_plan():
    heads = []
    col = 0
    mslots = {64: [], 256: []}
    mindex = {}
    for w in (256, 64):
        for v in range(4):
            for b in range(256 // w):
                qlo_s = 1024 * v + w * b
                qhi_s = 1024 * v + 768 + w * (b + 1)
                lod = qlo_s // 128
                kend = min((qhi_s + 127) // 128, 8 * (v + 1))
                for kt in range(lod, kend):
                    mindex[(w, v, b, kt)] = len(mslots[w])
                    mslots[w].append((v, b, kt))
    for h in range(1, NH + 1):
        s = head_slope(h)
        W = WIN / s
        w = head_w(h)
        blocks = []
        for v in range(4):
            for b in range(256 // w):
                qlo_s = 1024 * v + w * b
                qhi_s = 1024 * v + 768 + w * (b + 1)
                klo = max(0, int((qlo_s - W) // 128))
                kend = min((qhi_s + 127) // 128, 8 * (v + 1))
                lod = qlo_s // 128
                blocks.append(dict(v=v, b=b, klo=klo, kend=kend,
                                   col0=col,
                                   mindex={kt: mindex[(w, v, b, kt)]
                                           for kt in range(max(klo, lod), kend)}))
                col += kend - klo
        heads.append(dict(h=h, w=w, slope=s, blocks=blocks))
    return heads, col, mslots[256], mslots[64]


HEADS, NCOL, M256_SLOTS, M64_SLOTS = build_plan()
M256_BYV = [[i for i, (v, b, kt) in enumerate(M256_SLOTS) if v == vv] for vv in range(4)]
M64_BYV = [[i for i, (v, b, kt) in enumerate(M64_SLOTS) if v == vv] for vv in range(4)]
MASKBUF_COLS = max(max(len(x) for x in M256_BYV) * 256,
                   max(len(x) for x in M64_BYV) * 64)
HEAD_ORDER = [hd for hd in HEADS if hd["w"] == 256] + [hd for hd in HEADS if hd["w"] == 64]


class Builder:
    def __init__(self):
        self.nc = bass.Bass("TRN2", target_bir_lowering=False)
        self.ops = {k: [] for k in ("tensor", "scalar", "vector", "gpsimd", "sync")}
        self.cnt = {}
        self.stack = ExitStack()
        self.sems = {}

    def mksem(self, names):
        for n in names:
            self.sems[n] = self.stack.enter_context(self.nc.semaphore(n))
            self.cnt[n] = 0

    def sb(self, name, shape, dtype):
        return self.stack.enter_context(self.nc.sbuf_tensor(name, shape, dtype))

    def psum(self, name, shape, dtype):
        return self.stack.enter_context(self.nc.psum_tensor(name, shape, dtype))

    def op(self, eng, fn, sem=None, k=1):
        if sem is None:
            self.ops[eng].append(fn)
            return None
        self.cnt[sem] += k
        val = self.cnt[sem]
        h = self.sems[sem]
        self.ops[eng].append(lambda e, f=fn, hh=h, kk=k: f(e).then_inc(hh, kk))
        return val

    def wait(self, eng, sem, val):
        if val is None or val <= 0:
            return
        h = self.sems[sem]
        self.ops[eng].append(lambda e, hh=h, vv=val: e.wait_ge(hh, vv))


def build_program():
    B = Builder()
    nc = B.nc

    def din(name, shape, dtype=F32):
        return nc.dram_tensor(name, list(shape), dtype, kind="ExternalInput").ap()

    x_tok = din("x_tok", (TPC, D))
    x_T = din("x_T", (D, TPC))
    wq = din("wq", (NH, 128, 32, 128))
    wkv = din("wkv", (2, 128, 32, 128))
    wff = din("wff", (32, 128, 32, 128))
    wo = din("wo", (32, 128, 32, 128))
    a_t = din("a_t", (128, 32, 64))
    b_t = din("b_t", (64, D))
    gw_t = din("gw_t", (128, 32, 4))
    bvec = din("bvec", (128, 32))
    gbias = din("gbias", (4, 1))
    ab_t = din("ab_t", (2, NCOL))
    rt_t = din("rt_t", (2, 128))
    m256_t = din("m256_t", (128, len(M256_SLOTS) * 256), BF16)
    m64_t = din("m64_t", (128, len(M64_SLOTS) * 64), BF16)
    ssel_t = din("ssel_t", (4, 64), BF16)

    out_T = nc.dram_tensor("out_T", [D, TPC], F32, kind="ExternalOutput").ap()

    cc_in_k = nc.dram_tensor("cc_in_k", [128, TPC], BF16)
    cc_out_k = nc.dram_tensor("cc_out_k", [4 * 128, TPC], BF16)
    cc_in_v = nc.dram_tensor("cc_in_v", [TPC, 128], BF16)
    cc_out_v = nc.dram_tensor("cc_out_v", [4 * TPC, 128], BF16)

    xn = B.sb("xn", [128, 32 * TPC], BF16)
    attn = B.sb("attn", [128, 32 * TPC], BF16)
    xbuf = B.sb("xbuf", [128, 4096], F32)
    kvb = B.sb("kvb", [128, 4096 + 32 * 129], BF16)
    wpan = B.sb("wpan", [128, 8192], BF16)
    maskb = B.sb("maskb", [128, 2 * MASKBUF_COLS], BF16)
    mult = B.sb("mult", [128, NCOL], F32)
    pbuf = B.sb("pbuf", [128, 4 * 768], BF16)
    q_sb = B.sb("q_sb", [128, 1024], BF16)
    att_stage = B.sb("att_stage", [128, 512], BF16)
    ident_bf = B.sb("ident_bf", [128, 128], BF16)
    ident_f = B.sb("ident_f", [128, 128], F32)
    rt_sb = xbuf[0:2, 1024:1152]
    bmat_sb = B.sb("bmat_sb", [128, 32], F32)
    gb_sb = B.sb("gb_sb", [4, 1], F32)
    ssel_sb = B.sb("ssel_sb", [4, 64], BF16)
    um_sb = B.sb("um_sb", [64, 1024], BF16)
    stats = B.sb("stats", [128, 14], F32)
    epscol = stats[:, 13:14]

    K_sb = kvb[:, 0:4096]
    KVB_PITCH = 4096 + 32 * 129

    def vaug(kt):
        return kvb[:, 4096 + kt * 129: 4096 + (kt + 1) * 129]

    ps_A = B.psum("ps_A", [128, 768], F32)
    ps_B = B.psum("ps_B", [128, 768], F32)
    ps_at0 = B.psum("ps_at0", [128, 512], F32)
    ps_at1 = B.psum("ps_at1", [128, 512], F32)
    ps_q = B.psum("ps_q", [128, 512], F32)
    ps_tr = B.psum("ps_tr", [128, 512], BF16)

    B.mksem(["pe", "act", "dve", "pool", "d_const", "d_x", "d_pan0", "d_pan1",
             "d_mask0", "d_mask1", "d_xt0", "d_xt1", "d_out", "d_kv", "d_ccst",
             "cc", "d_ab0", "d_ab1"])
    C = B.cnt

    for (dst, src) in [(rt_sb[:], rt_t), (bmat_sb[:], bvec), (gb_sb[:], gbias),
                       (ssel_sb[:], ssel_t)]:
        B.op("sync", lambda e, d=dst, s=src: e.dma_start(out=d, in_=s), "d_const", 16)
    v_const = C["d_const"]

    wi1 = B.op("gpsimd", lambda e: e.memset(ident_bf[:], 1.0), "pool")
    B.wait("gpsimd", "pool", wi1)
    B.op("gpsimd", lambda e: e.affine_select(
        ident_bf[:], ident_bf[:], [[1, 128]], ALU.is_equal, 0.0,
        base=0, channel_multiplier=-1), "pool")
    wi2 = B.op("gpsimd", lambda e: e.memset(ident_f[:], 1.0), "pool")
    B.wait("gpsimd", "pool", wi2)
    B.op("gpsimd", lambda e: e.affine_select(
        ident_f[:], ident_f[:], [[1, 128]], ALU.is_equal, 0.0,
        base=0, channel_multiplier=-1), "pool")
    B.op("gpsimd", lambda e: e.memset(kvb[:, 4096:4096 + 32 * 129], 1.0), "pool")
    B.op("gpsimd", lambda e: e.memset(epscol[:], EPS), "pool")
    v_pool_idents = C["pool"]

    nchunks = (NCOL + 511) // 512
    B.wait("tensor", "d_const", v_const)
    for i in range(nchunks):
        c0, c1 = i * 512, min(NCOL, (i + 1) * 512)
        n = c1 - c0
        par = i % 2
        abch = xbuf[0:2, par * 512:(par + 1) * 512]
        dsem = "d_ab1" if par else "d_ab0"
        if i >= 2:
            B.wait("sync", "pe", C["pe"])
        B.op("sync", lambda e, d=abch[0:2, 0:n], s=ab_t[:, c0:c1]:
             e.dma_start(out=d, in_=s), dsem, 16)
        B.wait("tensor", dsem, C[dsem])
        if i >= 1:
            B.wait("tensor", "dve", C["dve"])
        B.op("tensor", lambda e, d=ps_A[:, 0:n], l=rt_sb[:], r=abch[0:2, 0:n]:
             e.matmul(d, l, r, start=True, stop=True), "pe")
        B.wait("vector", "pe", C["pe"])
        B.op("vector", lambda e, d=mult[:, c0:c1], s=ps_A[:, 0:n]:
             e.tensor_scalar(d, s, 0.0, None, ALU.min), "dve")
    B.wait("scalar", "dve", C["dve"])
    B.op("scalar", lambda e: e.activation(mult[:], mult[:], AF.Exp), "act")

    inv_d = 1.0 / D
    QSEM = ["d_x", "d_xt0", "d_xt1", "d_ab0"]
    v_apply = [[0] * 4 for _ in range(8)]
    v_sq = [[0] * 4 for _ in range(8)]
    v_pe_tr = [[0] * 4 for _ in range(8)]
    for t in range(8):
        if t == 0:
            B.wait("sync", "pe", C["pe"])
        vq_loaded = [0] * 4
        for qq in range(4):
            if t > 0:
                B.wait("sync", "dve", v_apply[t - 1][qq])
                B.wait("sync", "act", v_sq[t - 1][qq])
            B.op("sync", lambda e, tt=t, q3=qq: e.dma_start(
                out=xbuf[:, q3 * 1024:(q3 + 1) * 1024],
                in_=x_tok[tt * 128:(tt + 1) * 128, q3 * 1024:(q3 + 1) * 1024]),
                QSEM[qq], 16)
            vq_loaded[qq] = C[QSEM[qq]]
        for qq in range(4):
            B.wait("vector", QSEM[qq], vq_loaded[qq])
            B.op("vector", lambda e, q3=qq: e.reduce_sum(
                stats[:, q3:q3 + 1], xbuf[:, q3 * 1024:(q3 + 1) * 1024], axis=AX.X), "dve")
            B.wait("scalar", QSEM[qq], vq_loaded[qq])
            v_sq[t][qq] = B.op("scalar", lambda e, q3=qq: e.activation(
                attn[:, 8192 + q3 * 1024:8192 + (q3 + 1) * 1024],
                xbuf[:, q3 * 1024:(q3 + 1) * 1024], AF.Square,
                accum_out=stats[:, 4 + q3:5 + q3]), "act")
        vrd = C["dve"]
        B.wait("vector", "dve", vrd)
        v1 = B.op("vector", lambda e: e.tensor_tensor(stats[:, 8:9], stats[:, 0:1], stats[:, 1:2], ALU.add), "dve")
        B.op("vector", lambda e: e.tensor_tensor(stats[:, 10:11], stats[:, 2:3], stats[:, 3:4], ALU.add), "dve")
        B.wait("vector", "dve", C["dve"])
        B.op("vector", lambda e: e.tensor_tensor(stats[:, 8:9], stats[:, 8:9], stats[:, 10:11], ALU.add), "dve")
        B.wait("vector", "act", v_sq[t][3])
        B.op("vector", lambda e: e.tensor_tensor(stats[:, 9:10], stats[:, 4:5], stats[:, 5:6], ALU.add), "dve")
        B.op("vector", lambda e: e.tensor_tensor(stats[:, 10:11], stats[:, 6:7], stats[:, 7:8], ALU.add), "dve")
        B.wait("vector", "dve", C["dve"])
        B.op("vector", lambda e: e.tensor_tensor(stats[:, 9:10], stats[:, 9:10], stats[:, 10:11], ALU.add), "dve")
        B.wait("vector", "dve", C["dve"])
        B.op("vector", lambda e: e.tensor_scalar(stats[:, 8:9], stats[:, 8:9], inv_d, None, ALU.mult), "dve")
        B.op("vector", lambda e: e.tensor_scalar(stats[:, 9:10], stats[:, 9:10], inv_d, None, ALU.mult), "dve")
        B.wait("vector", "dve", C["dve"])
        B.op("vector", lambda e: e.tensor_tensor(stats[:, 10:11], stats[:, 8:9], stats[:, 8:9], ALU.mult), "dve")
        B.wait("vector", "dve", C["dve"])
        B.op("vector", lambda e: e.tensor_tensor(stats[:, 9:10], stats[:, 9:10], stats[:, 10:11], ALU.subtract), "dve")
        vv = C["dve"]
        B.wait("scalar", "dve", vv)
        if t == 0:
            B.wait("scalar", "pool", v_pool_idents)
        va1 = B.op("scalar", lambda e: e.activation(stats[:, 10:11], stats[:, 9:10], AF.Ln, bias=epscol[:]), "act")
        B.wait("scalar", "act", va1)
        B.op("scalar", lambda e: e.activation(stats[:, 11:12], stats[:, 10:11], AF.Exp, scale=-0.5), "act")
        vr = C["act"]
        slot = attn[:, (t % 2) * 4096:(t % 2) * 4096 + 4096]
        B.wait("vector", "act", vr)
        for qq in range(4):
            if t > 1:
                B.wait("vector", "pe", v_pe_tr[t - 2][qq])
            B.op("vector", lambda e, d=slot, q3=qq: e.tensor_scalar(
                d[:, q3 * 1024:(q3 + 1) * 1024], xbuf[:, q3 * 1024:(q3 + 1) * 1024],
                stats[:, 8:9], stats[:, 11:12], ALU.subtract, ALU.mult), "dve")
            v_apply[t][qq] = C["dve"]
            B.wait("tensor", "dve", v_apply[t][qq])
            for c0 in range(8 * qq, 8 * qq + 8, 4):
                B.wait("tensor", "dve", C["dve"])
                for j in range(4):
                    B.op("tensor", lambda e, s=slot[:, (c0 + j) * 128:(c0 + j + 1) * 128],
                         d=ps_tr[:, j * 128:(j + 1) * 128]:
                         e.transpose(d, s, ident_bf[:]), "pe")
                vp = C["pe"]
                B.wait("vector", "pe", vp)
                dst = AP(xn, c0 * 1024 + t * 128, [[32 * TPC, 128], [1024, 4], [1, 128]])
                B.op("vector", lambda e, d=dst: e.tensor_copy(d, ps_tr[:, 0:512]), "dve")
            v_pe_tr[t][qq] = C["pe"]
    v_p1_dve = C["dve"]

    B.wait("gpsimd", "pool", v_pool_idents)
    for f in range(2):
        B.op("gpsimd", lambda e, ff=f: e.dma_start(
            out=wpan[:, ff * 4096:(ff + 1) * 4096].rearrange("p (c f) -> p c f", c=32),
            in_=wkv[ff]), "d_pan0", 16)
    v_kvpan = C["d_pan0"]
    K_loc = attn[:, 0:1024]
    V_locT = attn[:, 1024:2048]
    V_tok = attn[:, 2048:3072]
    B.wait("tensor", "d_pan0", v_kvpan)
    B.wait("tensor", "dve", v_p1_dve)
    for f in range(2):
        for tb in range(2):
            for c in range(32):
                B.op("tensor", lambda e, ff=f, cc2=c, tt=tb: e.matmul(
                    ps_A[:, 0:512],
                    wpan[:, ff * 4096 + cc2 * 128: ff * 4096 + (cc2 + 1) * 128],
                    xn[:, cc2 * 1024 + tt * 512: cc2 * 1024 + (tt + 1) * 512],
                    start=(cc2 == 0), stop=(cc2 == 31)), "pe")
            vmm = C["pe"]
            B.wait("vector", "pe", vmm)
            dst = (K_loc if f == 0 else V_locT)[:, tb * 512:(tb + 1) * 512]
            B.op("vector", lambda e, d=dst: e.tensor_copy(d, ps_A[:, 0:512]), "dve")
            B.wait("tensor", "dve", C["dve"])
    for j0 in range(0, 8, 4):
        if j0 > 0:
            B.wait("tensor", "dve", C["dve"])
        for j in range(4):
            B.op("tensor", lambda e, s=V_locT[:, (j0 + j) * 128:(j0 + j + 1) * 128],
                 d=ps_tr[:, j * 128:(j + 1) * 128]:
                 e.transpose(d, s, ident_bf[:]), "pe")
        vp = C["pe"]
        B.wait("vector", "pe", vp)
        B.op("vector", lambda e, d=V_tok[:, j0 * 128:(j0 + 4) * 128]:
             e.tensor_copy(d, ps_tr[:, 0:512]), "dve")
    v_kv_done = C["dve"]
    B.wait("sync", "dve", v_kv_done)
    B.op("sync", lambda e: e.dma_start(out=cc_in_k.ap(), in_=K_loc), "d_ccst", 16)
    B.op("sync", lambda e: e.dma_start(
        out=cc_in_v.ap().rearrange("(j p) f -> p j f", p=128),
        in_=V_tok.rearrange("p (j f) -> p j f", j=8)), "d_ccst", 16)
    if os.environ.get("KDBG"):
        dbg_kloc = nc.dram_tensor("dbg_kloc", [128, 1024], BF16, kind="ExternalOutput").ap()
        dbg_vtok = nc.dram_tensor("dbg_vtok", [128, 1024], BF16, kind="ExternalOutput").ap()
        B.op("sync", lambda e: e.dma_start(out=dbg_kloc, in_=K_loc), "d_ccst", 16)
        B.op("sync", lambda e: e.dma_start(out=dbg_vtok, in_=V_tok), "d_ccst", 16)
    v_ccst = C["d_ccst"]
    B.wait("gpsimd", "d_ccst", v_ccst)
    B.op("gpsimd", lambda e: e.collective_compute(
        "AllGather", ALU.bypass, replica_groups=[[0, 1, 2, 3], [4, 5, 6, 7]],
        ins=[cc_in_k.ap().opt()], outs=[cc_out_k.ap().opt()]), "cc")
    B.op("gpsimd", lambda e: e.collective_compute(
        "AllGather", ALU.bypass, replica_groups=[[0, 1, 2, 3], [4, 5, 6, 7]],
        ins=[cc_in_v.ap().opt()], outs=[cc_out_v.ap().opt()]), "cc")
    v_cc = C["cc"]
    B.wait("sync", "cc", v_cc)
    for cseq in range(16):
        if cseq < 4:
            r, v = cseq, 0
        elif cseq < 8:
            r, v = 7 - cseq, 1
        elif cseq < 12:
            r, v = cseq - 8, 2
        else:
            r, v = 15 - cseq, 3
        B.op("sync", lambda e, d=K_sb[:, 256 * cseq:256 * (cseq + 1)],
             s=cc_out_k.ap()[128 * r:128 * (r + 1), 256 * v:256 * (v + 1)]:
             e.dma_start(out=d, in_=s), "d_kv", 16)
        dstv = AP(kvb, 4096 + (2 * cseq) * 129, [[KVB_PITCH, 128], [129, 2], [1, 128]])
        B.op("sync", lambda e, d=dstv,
             s=cc_out_v.ap()[1024 * r + 256 * v:1024 * r + 256 * (v + 1), :]
             .rearrange("(j p) f -> p j f", p=128):
             e.dma_start(out=d, in_=s), "d_kv", 16)
    v_kvload = C["d_kv"]

    A_sb = wpan[:, 0:2048]
    gw_sb = wpan[:, 2048:2176]
    B.wait("gpsimd", "pe", C["pe"])
    B.op("gpsimd", lambda e: e.dma_start(
        out=A_sb.rearrange("p (c r) -> p c r", c=32), in_=a_t), "d_pan1", 16)
    B.op("gpsimd", lambda e: e.dma_start(
        out=gw_sb.rearrange("p (c j) -> p c j", c=32), in_=gw_t), "d_pan1", 16)
    v_p3pan = C["d_pan1"]
    gate_T = xbuf[0:4, 0:1024]
    gtok = xbuf[:, 1024:1056]
    mstack = xbuf[:, 1056:1088]
    B.wait("tensor", "d_pan1", v_p3pan)
    for tb in range(2):
        for c in range(32):
            B.op("tensor", lambda e, cc2=c, tt=tb: e.matmul(
                ps_q[0:4, 0:512], gw_sb[:, cc2 * 4:(cc2 + 1) * 4],
                xn[:, cc2 * 1024 + tt * 512: cc2 * 1024 + (tt + 1) * 512],
                start=(cc2 == 0), stop=(cc2 == 31)), "pe")
        vg = C["pe"]
        B.wait("vector", "pe", vg)
        B.wait("vector", "d_const", v_const)
        B.op("vector", lambda e, d=gate_T[0:4, tb * 512:(tb + 1) * 512]:
             e.tensor_scalar(d, ps_q[0:4, 0:512], gb_sb[:], None, ALU.add), "dve")
        B.wait("tensor", "dve", C["dve"])
    B.op("gpsimd", lambda e: e.memset(mstack, 1.0), "pool")
    v_mset = C["pool"]
    B.wait("vector", "pool", v_mset)
    for t in range(8):
        if t > 0:
            B.wait("tensor", "dve", C["dve"])
        B.op("tensor", lambda e, s=gate_T[0:4, t * 128:(t + 1) * 128]:
             e.transpose(ps_q[:, 256:260], s, ident_f[0:4, 0:4]), "pe")
        vp = C["pe"]
        B.wait("vector", "pe", vp)
        vcp = B.op("vector", lambda e, d=gtok[:, t * 4:(t + 1) * 4]:
             e.tensor_copy(d, ps_q[:, 256:260]), "dve")
        B.wait("vector", "dve", vcp)
        g0 = gtok[:, t * 4 + 0:t * 4 + 1]
        g1 = gtok[:, t * 4 + 1:t * 4 + 2]
        g2 = gtok[:, t * 4 + 2:t * 4 + 3]
        g3 = gtok[:, t * 4 + 3:t * 4 + 4]
        sc = xbuf[:, 1088 + t * 4:1092 + t * 4]
        ms = mstack[:, t * 4:(t + 1) * 4]
        w1 = B.op("vector", lambda e, a=sc[:, 0:1], x=g1, y=g2: e.tensor_tensor(a, x, y, ALU.max), "dve")
        B.wait("vector", "dve", w1)
        w2 = B.op("vector", lambda e, a=sc[:, 0:1], y=g3: e.tensor_tensor(a, a, y, ALU.max), "dve")
        B.wait("vector", "dve", w2)
        B.op("vector", lambda e, d=ms[:, 1:2], x=sc[:, 0:1], y=g0: e.tensor_tensor(d, x, y, ALU.is_gt), "dve")
        w3 = B.op("vector", lambda e, a=sc[:, 1:2], x=g2, y=g3: e.tensor_tensor(a, x, y, ALU.max), "dve")
        w4 = B.op("vector", lambda e, a=sc[:, 2:3], x=g0, y=g1: e.tensor_tensor(a, x, y, ALU.max), "dve")
        B.wait("vector", "dve", w4)
        B.op("vector", lambda e, d=ms[:, 2:3], x=sc[:, 1:2], y=sc[:, 2:3]: e.tensor_tensor(d, x, y, ALU.is_gt), "dve")
        w5 = B.op("vector", lambda e, a=sc[:, 3:4], x=sc[:, 2:3], y=g2: e.tensor_tensor(a, x, y, ALU.max), "dve")
        B.wait("vector", "dve", w5)
        B.op("vector", lambda e, d=ms[:, 3:4], x=g3, y=sc[:, 3:4]: e.tensor_tensor(d, x, y, ALU.is_gt), "dve")
    v_mstack = C["dve"]
    msk_T = attn[0:4, 3072:4096]
    B.wait("tensor", "dve", v_mstack)
    for t in range(8):
        if t > 0:
            B.wait("tensor", "dve", C["dve"])
        B.op("tensor", lambda e, s=mstack[:, t * 4:(t + 1) * 4]:
             e.transpose(ps_q[0:4, 256:384], s, ident_f[:]), "pe")
        vp = C["pe"]
        B.wait("vector", "pe", vp)
        B.op("vector", lambda e, d=msk_T[0:4, t * 128:(t + 1) * 128]:
             e.tensor_copy(d, ps_q[0:4, 256:384]), "dve")
    v_mskT = C["dve"]
    B.wait("tensor", "dve", v_mskT)
    for tb in range(2):
        for c in range(32):
            B.op("tensor", lambda e, cc2=c, tt=tb: e.matmul(
                ps_A[0:64, 0:512], A_sb[:, cc2 * 64:(cc2 + 1) * 64],
                xn[:, cc2 * 1024 + tt * 512: cc2 * 1024 + (tt + 1) * 512],
                start=(cc2 == 0), stop=(cc2 == 31)), "pe")
        B.op("tensor", lambda e, tt=tb: e.matmul(
            ps_B[0:64, 0:512], ssel_sb[:], msk_T[0:4, tt * 512:(tt + 1) * 512],
            start=True, stop=True), "pe")
        vu = C["pe"]
        B.wait("vector", "pe", vu)
        w6 = B.op("vector", lambda e, d=um_sb[0:64, tb * 512:(tb + 1) * 512]:
             e.tensor_copy(d, ps_B[0:64, 0:512]), "dve")
        B.wait("vector", "dve", w6)
        B.op("vector", lambda e, d=um_sb[0:64, tb * 512:(tb + 1) * 512]:
             e.tensor_tensor(d, ps_A[0:64, 0:512], d, ALU.mult), "dve")
        B.wait("tensor", "dve", C["dve"])
    v_p3_done = C["dve"]

    v_pan = [0, 0]

    def emit_panel_load(hidx, par, wait_pe):
        hd = HEAD_ORDER[hidx]
        B.wait("gpsimd", "pe", wait_pe)
        B.op("gpsimd", lambda e, s=wq[hd["h"] - 1], pp=par: e.dma_start(
            out=wpan[:, pp * 4096:(pp + 1) * 4096].rearrange("p (c f) -> p c f", c=32),
            in_=s), "d_pan%d" % par, 16)
        v_pan[par] = C["d_pan%d" % par]

    mask_state = {}
    mask_info = {}
    par_m = [0]

    def emit_mask_load(w, v):
        p = par_m[0]
        par_m[0] = 1 - p
        old = mask_state.get(p)
        if old is not None:
            mask_info.pop(old, None)
        B.wait("sync", "dve", C["dve"])
        slots = M256_BYV[v] if w == 256 else M64_BYV[v]
        src_t = m256_t if w == 256 else m64_t
        n = len(slots) * w
        s0 = slots[0]
        B.op("sync", lambda e, d=maskb[:, p * MASKBUF_COLS: p * MASKBUF_COLS + n],
             s=src_t[:, s0 * w: s0 * w + n]:
             e.dma_start(out=d, in_=s), "d_mask%d" % p, 16)
        cmap = {sl: p * MASKBUF_COLS + (sl - s0) * w for sl in slots}
        mask_state[p] = (w, v)
        mask_info[(w, v)] = (C["d_mask%d" % p], cmap, p)

    grp_exp_done = []
    grp_mult_done = []
    grp_pv_done = {}
    pend_pv = []
    state = {"qevac": v_p3_done, "last_copy": 0, "ev": 0, "stage_free": 0,
             "bank_evac": [0, 0]}
    pend_fin = []
    at_banks = [ps_at0, ps_at1]

    def emit_evac(ev):
        mq, bank, col0 = ev["mq"], ev["bank"], ev["col0"]
        bk = at_banks[bank]
        B.wait("vector", "pe", ev["pvwait"])
        B.wait("vector", "pe", state["stage_free"])
        w7 = B.op("vector", lambda e, d=stats[0:mq, 12:13], s=bk[0:mq, col0 + 128:col0 + 129]:
                  e.reciprocal(d, s), "dve")
        B.wait("vector", "dve", w7)
        sc = ev["stage_col"]
        vts = B.op("vector", lambda e, d=att_stage[0:mq, sc:sc + 128],
                   s=bk[0:mq, col0:col0 + 128], r=stats[0:mq, 12:13]:
                   e.tensor_scalar(d, s, r, None, ALU.mult), "dve")
        state["bank_evac"][bank] = vts
        pend_fin.append((sc, mq, ev["dst"], vts))

    def flush_evac_fin():
        for (sc, mq, dst, vts) in pend_fin:
            B.wait("tensor", "dve", vts)
            B.wait("tensor", "dve", state["last_copy"])
            rot = state["ev"] % 2
            state["ev"] += 1
            B.op("tensor", lambda e, s=att_stage[0:mq, sc:sc + 128],
                 d=ps_tr[:, rot * 128:rot * 128 + mq], m=mq:
                 e.transpose(d, s, ident_bf[0:m, 0:m]), "pe")
            vtr = C["pe"]
            B.wait("vector", "pe", vtr)
            B.op("vector", lambda e, d=attn[:, dst:dst + mq],
                 s=ps_tr[:, rot * 128:rot * 128 + mq]: e.tensor_copy(d, s), "dve")
            state["last_copy"] = C["dve"]
        if pend_fin:
            state["stage_free"] = C["pe"]
        pend_fin.clear()

    def emit_pv_batch(item):
        B.wait("tensor", "dve", item["mult_done"])
        if item["memset_wait"]:
            B.wait("tensor", "dve", item["memset_wait"])
        for (post_col, mq, bank, outcol, kt) in item["pvs"]:
            B.op("tensor", lambda e, d=at_banks[bank][0:mq, outcol:outcol + 129],
                 l=pbuf[:, post_col:post_col + mq], r=vaug(kt):
                 e.matmul(d, l, r, start=False, stop=False, skip_group_check=True), "pe")
        pvd = C["pe"]
        grp_pv_done[item["gi"]] = pvd
        for ev in item["evacs"]:
            ev["pvwait"] = pvd
            emit_evac(ev)

    def flush_pv(upto):
        while len(pend_pv) > upto:
            emit_pv_batch(pend_pv.pop(0))

    def emit_group(gi, tiles, qr, dve_ops, pvs, evacs, memset_wait):
        gw_cols = len(tiles) * 256
        bankA = (gi % 2 == 0)
        psb = ps_A if bankA else ps_B
        post_off = (gi % 4) * 768
        pre = pbuf[:, post_off:post_off + 768]
        if gi >= 2:
            B.wait("tensor", "act", grp_exp_done[gi - 2])
        B.wait("tensor", "dve", state["qevac"])
        for ji, kt in enumerate(tiles):
            B.op("tensor", lambda e, d=psb[:, ji * 256:(ji + 1) * 256],
                 l=K_sb[:, kt * 128:(kt + 1) * 128], r=qr:
                 e.matmul(d, l, r, start=True, stop=True), "pe")
        v_sc = C["pe"]
        B.wait("scalar", "pe", v_sc)
        if gi >= 4:
            B.wait("scalar", "pe", grp_pv_done[gi - 4])
        B.op("scalar", lambda e, d=pre[:, 0:gw_cols], s=psb[:, 0:gw_cols]:
             e.activation(d, s, AF.Exp, scale=SCALE), "act")
        grp_exp_done.append(C["act"])
        B.wait("vector", "act", grp_exp_done[-1])
        for (ji, qoff, qw, colx, moff) in dve_ops:
            mcol = mult[:, colx:colx + 1]
            dd = pre[:, ji * 256 + qoff: ji * 256 + qoff + qw]
            if moff is not None:
                B.op("vector", lambda e, d=dd, s=dd, m=maskb[:, moff:moff + qw], c2=mcol:
                     e.scalar_tensor_tensor(d, s, c2, m, ALU.mult, ALU.mult), "dve")
            else:
                B.op("vector", lambda e, d=dd, s=dd, c2=mcol:
                     e.tensor_scalar(d, s, c2, None, ALU.mult), "dve")
        grp_mult_done.append(C["dve"])
        pend_pv.append(dict(gi=gi, mult_done=grp_mult_done[-1], memset_wait=memset_wait,
                            pvs=[(post_off + pc, mq, bank, oc, kt) for (pc, mq, bank, oc, kt) in pvs],
                            evacs=evacs))
        flush_pv(3)

    gi_all = 0
    emit_mask_load(HEAD_ORDER[0]["w"], 0)
    emit_panel_load(0, 0, C["pe"])

    for hidx, hd in enumerate(HEAD_ORDER):
        h, w, slope = hd["h"], hd["w"], hd["slope"]
        par = hidx % 2
        if hidx > 0 and HEAD_ORDER[hidx - 1]["w"] == 64:
            flush_pv(0)
            flush_evac_fin()
        B.wait("tensor", "d_pan%d" % par, v_pan[par])
        for qr_i in range(2):
            qps = ps_q if qr_i == 0 else ps_at0
            if qr_i == 0:
                B.wait("tensor", "dve", state["qevac"])
            else:
                B.wait("tensor", "dve", state["bank_evac"][0])
            for c in range(32):
                B.op("tensor", lambda e, cc2=c, rr=qr_i, pp=par, d=qps: e.matmul(
                    d[:, 0:512],
                    wpan[:, pp * 4096 + cc2 * 128: pp * 4096 + (cc2 + 1) * 128],
                    xn[:, cc2 * 1024 + rr * 512: cc2 * 1024 + rr * 512 + 512],
                    start=(cc2 == 0), stop=(cc2 == 31)), "pe")
            vq = C["pe"]
            B.wait("vector", "pe", vq)
            B.op("vector", lambda e, d=q_sb[:, qr_i * 512:(qr_i + 1) * 512], s=qps:
                 e.tensor_copy(d, s[:, 0:512]), "dve")
            state["qevac"] = C["dve"]
        if hidx + 1 < len(HEAD_ORDER):
            emit_panel_load(hidx + 1, 1 - par, C["pe"])
        if hidx == 0:
            B.wait("tensor", "d_kv", v_kvload)
        for v in range(4):
            if (w, v) not in mask_info:
                emit_mask_load(w, v)
            if v < 3 and (w, v + 1) not in mask_info:
                emit_mask_load(w, v + 1)
            msemv, mcolmap, mp = mask_info[(w, v)]
            B.wait("vector", "d_mask%d" % mp, msemv)
            qr = q_sb[:, v * 256:(v + 1) * 256]
            blks = [bb for bb in hd["blocks"] if bb["v"] == v]
            if w == 256:
                blk = blks[0]
                bank = v % 2
                ms = B.op("vector", lambda e, d=at_banks[bank][:, 0:258]: e.memset(d, 0.0), "dve")
                klist = list(range(blk["klo"], blk["kend"]))
                groups = [klist[i:i + 3] for i in range(0, len(klist), 3)]
                for gj, tiles in enumerate(groups):
                    dve_ops = []
                    pvs = []
                    for ji, kt in enumerate(tiles):
                        colx = blk["col0"] + (kt - blk["klo"])
                        moff = mcolmap[blk["mindex"][kt]] if kt in blk["mindex"] else None
                        dve_ops.append((ji, 0, 256, colx, moff))
                        for qs in range(2):
                            pvs.append((ji * 256 + qs * 128, 128, bank, qs * 129, kt))
                    evacs = []
                    if gj == len(groups) - 1:
                        for qs in range(2):
                            evacs.append(dict(mq=128, bank=bank, col0=qs * 129,
                                              stage_col=(v % 2) * 256 + qs * 128,
                                              dst=(h - 1) * 1024 + v * 256 + qs * 128))
                    emit_group(gi_all, tiles, qr, dve_ops, pvs, evacs, ms)
                    gi_all += 1
                    if gj == 0:
                        flush_evac_fin()
            else:
                flush_pv(0)
                ms0 = B.op("vector", lambda e, d=at_banks[0][0:64, 0:258]: e.memset(d, 0.0), "dve")
                ms1 = B.op("vector", lambda e, d=at_banks[1][0:64, 0:258]: e.memset(d, 0.0), "dve")
                klo_u = min(bb["klo"] for bb in blks)
                kend_u = max(bb["kend"] for bb in blks)
                klist = list(range(klo_u, kend_u))
                groups = [klist[i:i + 3] for i in range(0, len(klist), 3)]
                last_kt = {bb["b"]: bb["kend"] - 1 for bb in blks}
                for gj, tiles in enumerate(groups):
                    dve_ops = []
                    pvs = []
                    evacs = []
                    for ji, kt in enumerate(tiles):
                        for bb in blks:
                            b = bb["b"]
                            if not (bb["klo"] <= kt < bb["kend"]):
                                continue
                            colx = bb["col0"] + (kt - bb["klo"])
                            moff = mcolmap[bb["mindex"][kt]] if kt in bb["mindex"] else None
                            dve_ops.append((ji, b * 64, 64, colx, moff))
                            pvs.append((ji * 256 + b * 64, 64, b // 2, (b % 2) * 129, kt))
                            if kt == last_kt[b]:
                                evacs.append(dict(mq=64, bank=b // 2, col0=(b % 2) * 129,
                                                  stage_col=b * 128,
                                                  dst=(h - 1) * 1024 + v * 256 + b * 64))
                    emit_group(gi_all, tiles, qr, dve_ops, pvs, evacs, max(ms0, ms1))
                    gi_all += 1
                    if gj == 0:
                        flush_evac_fin()
        if hidx + 1 < len(HEAD_ORDER):
            nw = HEAD_ORDER[hidx + 1]["w"]
            if (nw, 0) not in mask_info:
                emit_mask_load(nw, 0)
    flush_pv(0)
    flush_evac_fin()
    v_attn_done = C["dve"]

    B_sbuf = maskb[0:64, 0:4096]
    B.wait("gpsimd", "dve", C["dve"])
    B.op("gpsimd", lambda e: e.dma_start(out=B_sbuf, in_=b_t), "d_mask0", 16)
    v_bt = C["d_mask0"]

    v_pan5 = [0, 0]

    def load_out_panels(ot, par, wait_pe):
        B.wait("gpsimd", "pe", wait_pe)
        B.op("gpsimd", lambda e, s=wo[ot], pp=par: e.dma_start(
            out=wpan[:, pp * 4096:(pp + 1) * 4096].rearrange("p (c f) -> p c f", c=32),
            in_=s), "d_pan%d" % par, 16)
        B.op("gpsimd", lambda e, s=wff[ot], pp=par: e.dma_start(
            out=kvb[:, pp * 4096:(pp + 1) * 4096].rearrange("p (c f) -> p c f", c=32),
            in_=s), "d_pan%d" % par, 16)
        v_pan5[par] = C["d_pan%d" % par]

    v_xt = [0, 0]

    def load_xt(ot, par, wait_dve):
        B.wait("sync", "dve", wait_dve)
        B.op("sync", lambda e, s=x_T[ot * 128:(ot + 1) * 128, :], pp=par:
             e.dma_start(out=xbuf[:, pp * 1024:(pp + 1) * 1024], in_=s), "d_xt%d" % par, 16)
        v_xt[par] = C["d_xt%d" % par]

    load_out_panels(0, 0, C["pe"])
    load_xt(0, 0, C["dve"])
    n_out = 0
    for ot in range(32):
        par = ot % 2
        if ot + 1 < 32:
            load_out_panels(ot + 1, 1 - par, C["pe"])
            load_xt(ot + 1, 1 - par, C["dve"])
        B.wait("tensor", "d_pan%d" % par, v_pan5[par])
        if ot == 0:
            B.wait("tensor", "dve", v_attn_done)
            B.wait("tensor", "d_mask0", v_bt)
        for tb in range(2):
            psb = ps_A if tb == 0 else ps_B
            B.wait("tensor", "dve", C["dve"])
            for c in range(32):
                B.op("tensor", lambda e, cc2=c, tt=tb, pp=par: e.matmul(
                    psb[:, 0:512] if False else (ps_A if tt == 0 else ps_B)[:, 0:512],
                    wpan[:, pp * 4096 + cc2 * 128: pp * 4096 + (cc2 + 1) * 128],
                    attn[:, cc2 * 1024 + tt * 512: cc2 * 1024 + (tt + 1) * 512],
                    start=(cc2 == 0), stop=False), "pe")
            for c in range(32):
                B.op("tensor", lambda e, cc2=c, tt=tb, pp=par: e.matmul(
                    (ps_A if tt == 0 else ps_B)[:, 0:512],
                    kvb[:, pp * 4096 + cc2 * 128: pp * 4096 + (cc2 + 1) * 128],
                    xn[:, cc2 * 1024 + tt * 512: cc2 * 1024 + (tt + 1) * 512],
                    start=False, stop=False), "pe")
            B.op("tensor", lambda e, oo=ot, tt=tb: e.matmul(
                (ps_A if tt == 0 else ps_B)[:, 0:512],
                B_sbuf[0:64, oo * 128:(oo + 1) * 128],
                um_sb[0:64, tt * 512:(tt + 1) * 512],
                start=False, stop=True), "pe")
            vmm = C["pe"]
            B.wait("vector", "pe", vmm)
            B.wait("vector", "d_xt%d" % par, v_xt[par])
            outsb = xbuf[:, 3072 + tb * 512: 3072 + (tb + 1) * 512]
            if n_out >= 2:
                B.wait("vector", "d_out", 16 * n_out)
            B.op("vector", lambda e, d=outsb, s=psb[:, 0:512], bcol=bmat_sb[:, ot:ot + 1],
                 xr=xbuf[:, par * 1024 + tb * 512: par * 1024 + (tb + 1) * 512]:
                 e.scalar_tensor_tensor(d, s, bcol, xr, ALU.add, ALU.add), "dve")
            vout = C["dve"]
            B.wait("sync", "dve", vout)
            B.op("sync", lambda e, s=outsb,
                 d=out_T[ot * 128:(ot + 1) * 128, tb * 512:(tb + 1) * 512]:
                 e.dma_start(out=d, in_=s), "d_out", 16)
            n_out += 1
    if os.environ.get("KDBG"):
        dbgs = [
            ("dbg_xn", xn[:, 0:1024], BF16, [128, 1024]),
            ("dbg_k", K_sb[:, 0:1024], BF16, [128, 1024]),
            ("dbg_v", kvb[:, 4096:4096 + 258], BF16, [128, 258]),
            ("dbg_q", q_sb[:], BF16, [128, 1024]),
            ("dbg_mult", mult[:, 0:512], F32, [128, 512]),
            ("dbg_attn_h9", attn[:, 8 * 1024:9 * 1024], BF16, [128, 1024]),
            ("dbg_attn_h1", attn[:, 0:1024], BF16, [128, 1024]),
            ("dbg_um", um_sb[:], BF16, [64, 1024]),
            ("dbg_gt", xbuf[:, 1024:1088], F32, [128, 64]),
            ("dbg_ppost", ppost[:], BF16, [128, 1536]),
            ("dbg_stats", stats[:], F32, [128, 16]),
        ]
        B.wait("sync", "dve", C["dve"])
        B.wait("sync", "pool", C["pool"])
        B.wait("sync", "act", C["act"])
        B.wait("sync", "pe", C["pe"])
        nd = 0
        for name, ap_, dt, shp in dbgs:
            o = nc.dram_tensor(name, shp, dt, kind="ExternalOutput").ap()
            B.op("sync", lambda e, d=o, s=ap_: e.dma_start(out=d, in_=s), "d_out", 16)
            nd += 1
        n_out += nd
    B.wait("sync", "d_out", 16 * n_out)

    with nc.Block() as block:
        @block.tensor
        def _(t):
            for f in B.ops["tensor"]:
                f(t)

        @block.scalar
        def _(s):
            for f in B.ops["scalar"]:
                f(s)

        @block.vector
        def _(v):
            for f in B.ops["vector"]:
                f(v)

        @block.gpsimd
        def _(g):
            for f in B.ops["gpsimd"]:
                f(g)

        @block.sync
        def _(sy):
            for f in B.ops["sync"]:
                f(sy)

    B.stack.close()
    return nc



def _stage_weights(Wqkv, Wo, W, A, Bm, gate_w, b, gate_b):
    f32 = np.float32
    wq_st = np.ascontiguousarray(
        np.transpose(Wqkv[:, :4096].reshape(32, 128, 32, 128), (2, 1, 0, 3)), dtype=f32)
    wkv_st = np.ascontiguousarray(
        np.transpose(Wqkv[:, 4096:].reshape(32, 128, 2, 128), (2, 1, 0, 3)), dtype=f32)
    wff_st = np.ascontiguousarray(
        np.transpose(W.reshape(32, 128, 32, 128), (0, 3, 2, 1)), dtype=f32)
    wo_st = np.ascontiguousarray(
        np.transpose(Wo.reshape(32, 128, 32, 128), (2, 1, 0, 3)), dtype=f32)
    a_st = np.ascontiguousarray(np.transpose(A.reshape(64, 32, 128), (2, 1, 0)), dtype=f32)
    b_st = np.ascontiguousarray(Bm.T, dtype=f32)
    gw_st = np.ascontiguousarray(np.transpose(gate_w.reshape(4, 32, 128), (2, 1, 0)), dtype=f32)
    bvec_st = np.ascontiguousarray(b.reshape(32, 128).T, dtype=f32)
    gb_st = np.ascontiguousarray(gate_b.reshape(4, 1), dtype=f32)
    return wq_st, wkv_st, wff_st, wo_st, a_st, b_st, gw_st, bvec_st, gb_st


def _core_tables(chunks):
    ab = np.zeros((2, NCOL), np.float32)
    for hd in HEADS:
        s, w = hd["slope"], hd["w"]
        for blk in hd["blocks"]:
            v, b = blk["v"], blk["b"]
            q_end = 256 * chunks[v] + w * (b + 1)
            for i, kt in enumerate(range(blk["klo"], blk["kend"])):
                ab[0, blk["col0"] + i] = s * (128 * kt - q_end)
                ab[1, blk["col0"] + i] = s
    bf16 = ml_dtypes.bfloat16
    m256 = np.zeros((128, len(M256_SLOTS) * 256), bf16)
    for i, (v, b, kt) in enumerate(M256_SLOTS):
        keys = 128 * kt + np.arange(128)[:, None]
        qs = 256 * chunks[v] + 256 * b + np.arange(256)[None, :]
        m256[:, i * 256:(i + 1) * 256] = (keys <= qs).astype(bf16)
    m64 = np.zeros((128, len(M64_SLOTS) * 64), bf16)
    for i, (v, b, kt) in enumerate(M64_SLOTS):
        keys = 128 * kt + np.arange(128)[:, None]
        qs = 256 * chunks[v] + 64 * b + np.arange(64)[None, :]
        m64[:, i * 64:(i + 1) * 64] = (keys <= qs).astype(bf16)
    return ab, m256, m64


def kernel(L0_attn_hidden_states, L0_attn_Wqkv, L0_attn_Wo, L0_ffn_W, L0_ffn_b,
           L0_ffn_A, L0_ffn_B, L0_ffn_gate_w, L0_ffn_gate_b):
    x = np.asarray(L0_attn_hidden_states, np.float32)
    (wq_st, wkv_st, wff_st, wo_st, a_st, b_st, gw_st, bvec_st, gb_st) = _stage_weights(
        np.asarray(L0_attn_Wqkv, np.float32), np.asarray(L0_attn_Wo, np.float32),
        np.asarray(L0_ffn_W, np.float32), np.asarray(L0_ffn_A, np.float32),
        np.asarray(L0_ffn_B, np.float32), np.asarray(L0_ffn_gate_w, np.float32),
        np.asarray(L0_ffn_b, np.float32), np.asarray(L0_ffn_gate_b, np.float32))
    rt_st = np.stack([np.ones(128, np.float32),
                      np.arange(128, dtype=np.float32)])
    ssel = np.zeros((4, 64), ml_dtypes.bfloat16)
    ssel[0, 0:8] = 1
    ssel[1, 8:16] = 1
    ssel[2, 16:32] = 1
    ssel[3, 32:64] = 1

    nc = build_program()

    in_maps = []
    tok_idx_all = []
    for slot in range(NCORE):
        bt, g = slot // 4, slot % 4
        chunks = [g, 7 - g, 8 + g, 15 - g]
        tok_idx = np.concatenate([np.arange(256 * c, 256 * (c + 1)) for c in chunks])
        tok_idx_all.append((bt, tok_idx))
        x_core = np.ascontiguousarray(x[bt, tok_idx, :])
        xT_core = np.ascontiguousarray(x_core.T)
        ab, m256, m64 = _core_tables(chunks)
        in_maps.append({
            "x_tok": x_core, "x_T": xT_core,
            "wq": wq_st, "wkv": wkv_st, "wff": wff_st, "wo": wo_st,
            "a_t": a_st, "b_t": b_st, "gw_t": gw_st, "bvec": bvec_st, "gbias": gb_st,
            "ab_t": ab, "rt_t": rt_st, "m256_t": m256, "m64_t": m64, "ssel_t": ssel,
        })

    res = run_bass_kernel_spmd(nc, in_maps, core_ids=list(range(NCORE)))

    out = np.empty((BATCH, SEQ, D), np.float32)
    for slot in range(NCORE):
        bt, tok_idx = tok_idx_all[slot]
        out[bt, tok_idx, :] = res.results[slot]["out_T"].T
    return out


if __name__ == "__main__":
    build_program()
    print("program built ok")
